# version 02
# baseline (speedup 1.0000x reference)
import numpy as np
import ml_dtypes
import concourse.bass as bass
import concourse.mybir as mybir
from concourse.bass_utils import run_bass_kernel_spmd

F32 = mybir.dt.float32
BF16 = mybir.dt.bfloat16
FP8 = mybir.dt.float8e4
Alu = mybir.AluOpType
Act = mybir.ActivationFunctionType
Ax = mybir.AxisListType

C = 2048
NT = 1024
NH = 512
KC = 16
HC = 16
D = 128
L = 64
IC = 48
GSZ = 12
NG = IC // GSZ
SCALE = float(1.0 / np.sqrt(128.0))
NEG = -1.0e9
EPS = 1e-5


class Plan:
    def __init__(self):
        self.ops = []
        self.cnt = {}

    def bump(self, sem, amt):
        self.cnt[sem] = self.cnt.get(sem, 0) + amt
        return self.cnt[sem]

    def at(self, sem):
        return self.cnt.get(sem, 0)

    def add(self, eng, fn, waits=(), inc=None):
        incs = []
        tgt = None
        if inc is not None:
            sem, amt = inc
            tgt = self.bump(sem, amt)
            incs = [(sem, amt)]
        self.ops.append((eng, tuple(waits), fn, tuple(incs)))
        return tgt


ESEM = {"vector": "dv", "scalar": "ac", "tensor": "mm"}
QSEM = {"sync": "sq", "gpsimd": "gq"}


def build():
    nc = bass.Bass("TRN2", target_bir_lowering=False)
    p = Plan()

    def dram_in(name, shape, dt=F32):
        return nc.dram_tensor(name, shape, dt, kind="ExternalInput").ap()

    xT = dram_in("xT", [C, NT])
    xhT = dram_in("xhT", [C, 128])
    wk = dram_in("wk", [C, C])
    wv = dram_in("wv", [C, C])
    wg = dram_in("wg", [C, HC * L])
    wp = dram_in("wp", [C, C])
    wgate = dram_in("wgate", [IC * 128, C])
    wup = dram_in("wup", [IC * 128, C])
    wdown = dram_in("wdown", [NG * 16 * 128, GSZ * 128])
    lqd = dram_in("lq", [128, HC * L])
    ropec = dram_in("ropec", [128, 128])
    ropes = dram_in("ropes", [128, 128])
    maska = dram_in("maska", [64, 128])
    idf = dram_in("idf", [128, 128])
    gsd = dram_in("gs", [128, IC])
    usd = dram_in("us", [128, IC])
    dsd = dram_in("ds", [128, KC])
    out = nc.dram_tensor("out", [C, NT], F32, kind="ExternalOutput").ap()

    import contextlib
    st = contextlib.ExitStack()
    sb = lambda name, shape, dt: st.enter_context(nc.sbuf_tensor(name, shape, dt))
    psb = lambda name, shape, dt: st.enter_context(nc.psum_tensor(name, shape, dt))

    with st:
        x_r = sb("x_r", [128, KC * NT], F32)
        xs_r = sb("xs_r", [128, KC * NT], BF16)
        xh_r = sb("xh_r", [128, KC * 128], F32)
        xsh_r = sb("xsh_r", [128, KC * 128], BF16)
        big = sb("big", [128, KC * NT], BF16)
        gFM = sb("gFM", [128, 8 * NT], BF16)
        wfa = sb("wfa", [128, 2048], F32)
        wfb = sb("wfb", [128, 2048], F32)
        wqa = sb("wqa", [128, 2048], BF16)
        wqb = sb("wqb", [128, 2048], BF16)
        qi32a = sb("qi32a", [128, 512], mybir.dt.int32)
        qi32b = sb("qi32b", [128, 512], mybir.dt.int32)
        scr1 = sb("scr1", [128, NT], F32)
        scr3 = sb("scr3", [128, 128], F32)
        scr4 = sb("scr4", [128, 128], F32)
        lqb = sb("lqb", [128, HC * L], BF16)
        rC = sb("rC", [128, 128], F32)
        rS = sb("rS", [128, 128], F32)
        mskt = sb("mskt", [64, 128], F32)
        idft = sb("idft", [128, 128], F32)
        ksw = sb("ksw", [128, 128], F32)
        k_sb = sb("k_sb", [128, HC * 128], BF16)
        v_sb = sb("v_sb", [128, HC * 128], BF16)

        wT_sb = sb("wT_sb", [128, 2 * 64], BF16)
        attn_sb = sb("attn_sb", [128, 8 * 128], BF16)
        atmp = sb("atmp", [64, 128], BF16)
        vech = sb("vech", [4, 128], F32)
        vec_hm, vec_hr, vec_h1, vec_h2 = (vech[0:1, :], vech[1:2, :],
                                          vech[2:3, :], vech[3:4, :])
        ones_c = sb("ones_c", [128, 1], F32)
        eps_t = sb("eps_t", [128, 1], F32)
        ones_r2 = sb("ones_r2", [1, 128], F32)
        ones_r = ones_r2[0:1, :]
        gs_t = sb("gs_t", [128, IC], F32)
        us_t = sb("us_t", [128, IC], F32)
        ds_t = sb("ds_t", [128, KC], F32)
        mx8 = sb("mx8", [128, 8], F32)
        nm8 = sb("nm8", [128, 8], F32)
        sm8 = sb("sm8", [128, 8], F32)
        rs8 = sb("rs8", [128, 8], F32)
        sc_mx = sb("sc_mx", [64, 1], F32)
        sc_nm = sb("sc_nm", [64, 1], F32)
        sc_s = sb("sc_s", [64, 1], F32)
        sc_r = sb("sc_r", [64, 1], F32)

        kt1 = scr3
        smt = scr3[0:64, :]
        ewt = scr4[0:64, :]
        wfp = ksw[0:64, :]
        ps_big = psb("ps_big", [128, 2048], F32)
        ps_b1 = psb("ps_b1", [128, 1024], F32)
        ps_b2 = psb("ps_b2", [128, 512], F32)
        ps_b3 = psb("ps_b3", [128, 512], F32)

        def dma(q, out_ap, in_ap, waits=()):
            fn = (lambda o, i: (lambda eng: eng.dma_start(out=o, in_=i)))(out_ap, in_ap)
            return p.add(q, fn, waits=waits, inc=(QSEM[q], 16))

        def op(eng, fn, waits=()):
            return p.add(eng, fn, waits=waits, inc=(ESEM[eng], 1))

        d_lq = dma("gpsimd", wfa[:, 0:HC * L], lqd[:, :])
        dma("gpsimd", rC[:, :], ropec[:, :])
        dma("gpsimd", rS[:, :], ropes[:, :])
        dma("gpsimd", mskt[:, :], maska[:, :])
        dma("gpsimd", idft[:, :], idf[:, :])
        dma("gpsimd", gs_t[:, :], gsd[:, :])
        dma("gpsimd", us_t[:, :], usd[:, :])
        dma("gpsimd", ds_t[:, :], dsd[:, :])
        d_xh = [dma("sync", xh_r[:, c * 128:(c + 1) * 128], xhT[c * 128:(c + 1) * 128, :])
                for c in range(KC)]
        d_x = [dma("sync", x_r[:, c * NT:(c + 1) * NT], xT[c * 128:(c + 1) * 128, :])
               for c in range(KC)]

        op("vector", lambda eng: eng.memset(ones_c[:, :], 1.0))
        op("vector", lambda eng: eng.memset(ones_r, 1.0))
        op("vector", lambda eng: eng.memset(eps_t[:, :], EPS))
        c_ones = p.at("dv")
        op("vector", lambda eng: eng.tensor_copy(lqb[:, :], wfa[:, 0:HC * L]),
           waits=[("gq", d_lq)])
        slab0_init = p.at("dv")

        sq_done = []
        for c in range(KC):
            w = [("sq", d_x[c])]
            if c >= 1:
                w.append(("mm", sq_done[c - 1]))
            scr = scr1
            op("scalar", (lambda cc, s: (lambda eng: eng.activation(
                s[:, :], x_r[:, cc * NT:(cc + 1) * NT], Act.Square)))(c, scr), waits=w)
            a_sq = p.at("ac")
            last = None
            for h in range(2):
                def mmfn(cc, s, hh):
                    def f(eng):
                        eng.matmul(ps_big[0:1, hh * NH:(hh + 1) * NH], ones_c[:, :],
                                   x_r[:, cc * NT + hh * NH: cc * NT + (hh + 1) * NH],
                                   start=(cc == 0), stop=(cc == KC - 1))
                        return eng.matmul(ps_big[0:1, 1024 + hh * NH:1024 + (hh + 1) * NH],
                                          ones_c[:, :], s[:, hh * NH:(hh + 1) * NH],
                                          start=(cc == 0), stop=(cc == KC - 1))
                    return f
                wmm = [("ac", a_sq)] + ([("dv", c_ones)] if c == 0 else [])
                last = op("tensor", mmfn(c, scr, h), waits=wmm)
            sq_done.append(last)
        mm_stats = p.at("mm")

        hs_done = []
        for c in range(KC):
            w = [("sq", d_xh[c])]
            if c >= 1:
                w.append(("mm", hs_done[c - 1]))
            op("scalar", (lambda cc: (lambda eng: eng.activation(
                scr3[:, :], xh_r[:, cc * 128:(cc + 1) * 128], Act.Square)))(c), waits=w)
            a_sq = p.at("ac")

            def mmfn(cc):
                def f(eng):
                    eng.matmul(ps_b1[0:1, 0:128], ones_c[:, :],
                               xh_r[:, cc * 128:(cc + 1) * 128],
                               start=(cc == 0), stop=(cc == KC - 1))
                    return eng.matmul(ps_b1[0:1, 128:256], ones_c[:, :], scr3[:, :],
                                      start=(cc == 0), stop=(cc == KC - 1))
                return f
            hs_done.append(op("tensor", mmfn(c), waits=[("ac", a_sq)]))
        mm_hstats = p.at("mm")

        inv_n = 1.0 / float(C)
        A = wfa[0:1, 0:1024]
        B = scr1[0:1, 0:1024]

        op("vector", lambda eng: eng.tensor_scalar(A, ps_big[0:1, 0:1024], inv_n,
                                                   None, Alu.mult),
           waits=[("mm", mm_hstats)])
        c_m = p.at("dv")

        def bcast_m(eng):
            r = None
            for h in range(2):
                r = eng.matmul(ps_big[:, h * NH:(h + 1) * NH], ones_r,
                               wfa[0:1, h * NH:(h + 1) * NH], start=True, stop=True)
            return r
        mm_bm = op("tensor", bcast_m, waits=[("dv", c_m)])

        def fin_v(eng):
            eng.tensor_scalar(B, ps_big[0:1, 1024:2048], inv_n, None, Alu.mult)
            eng.tensor_tensor(A, A, A, Alu.mult)
            return eng.tensor_tensor(B, B, A, Alu.subtract)
        op("vector", fin_v, waits=[("mm", mm_bm)])
        c_var = p.at("dv")
        op("scalar", lambda eng: eng.activation(A, B, Act.Sqrt, bias=eps_t[0:1, :]),
           waits=[("dv", c_var)])
        a_std = p.at("ac")
        op("vector", lambda eng: eng.reciprocal(B, A), waits=[("ac", a_std)])
        c_r = p.at("dv")

        def bcast_r(eng):
            r = None
            for h in range(2):
                r = eng.matmul(ps_big[:, 1024 + h * NH:1024 + (h + 1) * NH], ones_r,
                               scr1[0:1, h * NH:(h + 1) * NH], start=True, stop=True)
            return r
        mm_br = op("tensor", bcast_r, waits=[("dv", c_r)])

        A2 = vech[0:1, :]
        B2 = scr1[0:1, 0:128]
        op("vector", lambda eng: eng.tensor_scalar(A2, ps_b1[0:1, 0:128], inv_n,
                                                   None, Alu.mult),
           waits=[("mm", mm_br)])
        c_hm = p.at("dv")
        mm_bhm = op("tensor", lambda eng: eng.matmul(ps_b2[:, 0:128], ones_r, A2,
                                                     start=True, stop=True),
                    waits=[("dv", c_hm)])

        def fin_hv(eng):
            eng.tensor_scalar(B2, ps_b1[0:1, 128:256], inv_n, None, Alu.mult)
            eng.tensor_tensor(A2, A2, A2, Alu.mult)
            return eng.tensor_tensor(B2, B2, A2, Alu.subtract)
        op("vector", fin_hv, waits=[("mm", mm_bhm)])
        c_hv = p.at("dv")
        op("scalar", lambda eng: eng.activation(A2, B2, Act.Sqrt, bias=eps_t[0:1, :]),
           waits=[("dv", c_hv)])
        a_hstd = p.at("ac")
        op("vector", lambda eng: eng.reciprocal(B2, A2), waits=[("ac", a_hstd)])
        c_hr = p.at("dv")
        mm_bcast = op("tensor", lambda eng: eng.matmul(ps_b2[:, 128:256], ones_r, B2,
                                                       start=True, stop=True),
                      waits=[("dv", c_hr)])

        for c in range(KC):
            def xnfn(cc):
                def f(eng):
                    eng.tensor_tensor(scr1[:, :], x_r[:, cc * NT:(cc + 1) * NT],
                                      ps_big[:, 0:1024], Alu.subtract)
                    return eng.tensor_tensor(xs_r[:, cc * NT:(cc + 1) * NT], scr1[:, :],
                                             ps_big[:, 1024:2048], Alu.mult)
                return f
            op("vector", xnfn(c), waits=[("mm", mm_bcast)] if c == 0 else ())
        for c in range(KC):
            def xnhfn(cc):
                def f(eng):
                    eng.tensor_tensor(scr3[:, :], xh_r[:, cc * 128:(cc + 1) * 128],
                                      ps_b2[:, 0:128], Alu.subtract)
                    return eng.tensor_tensor(xsh_r[:, cc * 128:(cc + 1) * 128], scr3[:, :],
                                             ps_b2[:, 128:256], Alu.mult)
                return f
            op("vector", xnhfn(c))
        c_xn = p.at("dv")

        slab_state = {"dv": [slab0_init, 0], "mm": [0, 0]}
        QAF = [xh_r[:, 0:512], xh_r[:, 1024:1536]]
        QX = [xh_r[:, 512:1024], xh_r[:, 1536:2048]]
        QB = [k_sb, v_sb]
        QI = [qi32a, qi32b]
        qstate = {"d2": [0, 0], "nxt": 0}
        T1, T2, T3, T4 = 2.0 ** -5, 2.0 ** -4, 2.0 ** -3, 2.0 ** -2
        EPS24 = 2.0 ** -24

        def quant_piece(wf, wq, c0, setid, first_waits=()):
            qaf, qx, qi = QAF[setid], QX[setid], QI[setid]
            bb = QB[setid]
            z1, z2, z3, z4 = (bb[:, 0:512], bb[:, 512:1024],
                              bb[:, 1024:1536], bb[:, 1536:2048])
            w1 = list(first_waits)
            if qstate["d2"][setid]:
                w1.append(("dv", qstate["d2"][setid]))
            a1 = op("scalar", (lambda w_, q_: (lambda eng: eng.activation(
                q_, w_, Act.Abs)))(wf[:, c0:c0 + 512], qaf), waits=w1)

            def casc(w_, q_):
                def f(eng):
                    eng.tensor_scalar(z1, q_, T1, -256.0, Alu.is_ge, Alu.mult)
                    eng.tensor_scalar(z2, q_, T2, -128.0, Alu.is_ge, Alu.mult)
                    eng.scalar_tensor_tensor(z1, z1, 512.0, z2, Alu.add, Alu.add)
                    eng.tensor_scalar(z3, q_, T3, -64.0, Alu.is_ge, Alu.mult)
                    eng.tensor_scalar(z4, q_, T4, -32.0, Alu.is_ge, Alu.mult)
                    eng.tensor_tensor(z3, z3, z4, Alu.add)
                    eng.tensor_tensor(z2, z1, z3, Alu.add)
                    return eng.tensor_tensor(qx, w_, z2, Alu.mult)
                return f
            d1 = op("vector", casc(wf[:, c0:c0 + 512], qaf), waits=[("ac", a1)])

            def convs(eng):
                eng.activation(qi[:, :], qx, Act.Copy, bias=1024.0)
                eng.activation(qaf, qi[:, :], Act.Copy)
                return eng.activation(qi[:, :], qi[:, :], Act.Copy,
                                      bias=0.25, scale=0.5)
            a2 = op("scalar", convs, waits=[("dv", d1)])

            def corr(q_out):
                def f(eng):
                    eng.scalar_tensor_tensor(z4, qi[:, :], -2.0, qaf, Alu.mult,
                                             Alu.add)
                    eng.scalar_tensor_tensor(qx, qaf, -1024.0, qx,
                                             Alu.add, Alu.subtract)
                    eng.scalar_tensor_tensor(qx, z4, -EPS24, qx, Alu.mult,
                                             Alu.add)
                    eng.tensor_scalar(z1, qx, -0.5, None, Alu.is_lt)
                    eng.tensor_scalar(z3, qx, 0.5, None, Alu.is_gt)
                    eng.scalar_tensor_tensor(qx, qaf, -1024.0, z1,
                                             Alu.add, Alu.add)
                    eng.tensor_tensor(qx, qx, z3, Alu.subtract)
                    eng.reciprocal(qaf, z2)
                    eng.drain()
                    return eng.tensor_tensor(q_out, qx, qaf, Alu.mult)
                return f
            d2 = op("vector", corr(wq[:, c0:c0 + 512]), waits=[("ac", a2)])
            qstate["d2"][setid] = d2
            return d2

        def stream_slab(src_ap, ncols, slot, quantize, extra_waits=()):
            wf = (wfa, wfb)[slot]
            wq = (wqa, wqb)[slot]
            w = list(extra_waits)
            if slab_state["dv"][slot]:
                w.append(("dv", slab_state["dv"][slot]))
            d = dma("sync", wf[:, 0:ncols], src_ap, waits=w)
            w2 = [("sq", d)]
            if slab_state["mm"][slot]:
                w2.append(("mm", slab_state["mm"][slot]))
            if quantize:
                cv = 0
                np_ = ncols // 512
                for pi in range(np_):
                    cv = quant_piece(wf, wq, pi * 512, qstate["nxt"] % 2,
                                     first_waits=(w2 if pi == 0 else ()))
                    qstate["nxt"] += 1
                slab_state["dv"][slot] = cv
                return wq, cv

            def cast(eng):
                return eng.tensor_copy(wq[:, 0:ncols], wf[:, 0:ncols])
            cv = op("vector", cast, waits=w2)
            slab_state["dv"][slot] = cv
            return wq, cv

        def slab_used(slot, mm_count):
            slab_state["mm"][slot] = mm_count

        for h in range(HC):
            wqv, cv = stream_slab(wk[h * 128:(h + 1) * 128, :], 2048, h % 2, False,
                                  extra_waits=[("mm", mm_bcast)] if h < 2 else ())

            def kmm(hh, wt):
                def f(eng):
                    r = None
                    for kc in range(KC):
                        r = eng.matmul(ps_big[:, hh * 128:(hh + 1) * 128],
                                       wt[:, kc * 128:(kc + 1) * 128],
                                       xsh_r[:, kc * 128:(kc + 1) * 128],
                                       start=(kc == 0), stop=(kc == KC - 1))
                    return r
                return f
            m = op("tensor", kmm(h, wqv), waits=[("dv", cv)])
            slab_used(h % 2, m)
        mm_k = p.at("mm")

        def vb(h):
            if h < 8:
                return ps_b1[:, h * 128:(h + 1) * 128]
            if h < 12:
                return ps_b2[:, (h - 8) * 128:(h - 7) * 128]
            return ps_b3[:, (h - 12) * 128:(h - 11) * 128]

        for h in range(HC):
            wqv, cv = stream_slab(wv[h * 128:(h + 1) * 128, :], 2048, h % 2, False)

            def vmm(hh, wt):
                def f(eng):
                    r = None
                    for kc in range(KC):
                        r = eng.matmul(vb(hh), xsh_r[:, kc * 128:(kc + 1) * 128],
                                       wt[:, kc * 128:(kc + 1) * 128],
                                       start=(kc == 0), stop=(kc == KC - 1))
                    return r
                return f
            m = op("tensor", vmm(h, wqv), waits=[("dv", cv)])
            slab_used(h % 2, m)
        mm_v = p.at("mm")

        for h in range(HC):
            op("scalar", (lambda hh: (lambda eng: eng.activation(
                v_sb[:, hh * 128:(hh + 1) * 128], vb(hh), Act.Copy)))(h),
               waits=[("mm", mm_v)] if h == 0 else ())
        a_vevac = p.at("ac")

        rope_done = []
        for h in range(HC):
            w = [("mm", mm_k), ("ac", a_vevac)] if h == 0 else [("dv", rope_done[-1])]
            a_kf = op("scalar", (lambda hh: (lambda eng: eng.activation(
                scr4[:, :], ps_big[:, hh * 128:(hh + 1) * 128], Act.Copy)))(h), waits=w)
            dsw1 = dma("gpsimd", ksw[0:64, :], scr4[64:128, :], waits=[("ac", a_kf)])
            dsw2 = dma("gpsimd", ksw[64:128, :], scr4[0:64, :])

            def rope(hh):
                def f(eng):
                    eng.tensor_tensor(kt1[:, :], scr4[:, :], rC[:, :], Alu.mult)
                    eng.tensor_tensor(ksw[:, :], ksw[:, :], rS[:, :], Alu.mult)
                    return eng.tensor_tensor(k_sb[:, hh * 128:(hh + 1) * 128],
                                             kt1[:, :], ksw[:, :], Alu.add)
                return f
            rope_done.append(op("vector", rope(h), waits=[("gq", dsw2)]))
        c_ropes = p.at("dv")

        attn_tail = []
        shift_dmas = []
        for h in range(HC):
            sps = (ps_b2, ps_b3)[h % 2]
            w = [("dv", c_ropes)] if h == 0 else []
            if h >= 2:
                w.append(("dv", attn_tail[h - 2]))
            mm_sc = op("tensor", (lambda hh, sp: (lambda eng: eng.matmul(
                sp[0:64, 0:128], lqb[:, hh * L:(hh + 1) * L],
                k_sb[:, hh * 128:(hh + 1) * 128], start=True, stop=True)))(h, sps),
                waits=w)

            def smax(hh, sp):
                def f(eng):
                    eng.scalar_tensor_tensor(smt, sp[0:64, 0:128], SCALE, mskt[:, :],
                                             Alu.mult, Alu.add)
                    eng.reduce_max(sc_mx[:, :], smt, Ax.X)
                    eng.drain()
                    return eng.tensor_scalar(sc_nm[:, :], sc_mx[:, :], -1.0, None, Alu.mult)
                return f
            c_sm = op("vector", smax(h, sps), waits=[("mm", mm_sc)])
            a_exp = op("scalar", lambda eng: eng.activation(ewt, smt, Act.Exp,
                                                            bias=sc_nm[:, :]),
                       waits=[("dv", c_sm)])

            def norm(eng):
                eng.reduce_sum(sc_s[:, :], ewt, Ax.X)
                eng.drain()
                return eng.reciprocal(sc_r[:, :], sc_s[:, :])
            c_nrm = op("vector", norm, waits=[("ac", a_exp)])
            a_wn = op("scalar", lambda eng: eng.activation(wfp, ewt, Act.Copy,
                                                           scale=sc_r[:, :]),
                      waits=[("dv", c_nrm)])
            tcol = (h % 2) * 512
            mm_t = op("tensor", (lambda tc: (lambda eng: eng.matmul(
                ps_big[0:128, tc:tc + 64], wfp, idft[0:64, 0:64], start=True, stop=True,
                is_transpose=True)))(tcol), waits=[("ac", a_wn)])
            c_wt = op("vector", (lambda hh, tc: (lambda eng: eng.tensor_copy(
                wT_sb[:, (hh % 2) * 64:(hh % 2 + 1) * 64], ps_big[0:128, tc:tc + 64])))(h, tcol),
                waits=[("mm", mm_t)])
            acol = 1024 + (h % 2) * 512
            mm_a = op("tensor", (lambda hh, cl: (lambda eng: eng.matmul(
                ps_big[0:64, cl:cl + 128], wT_sb[:, (hh % 2) * 64:(hh % 2 + 1) * 64],
                v_sb[:, hh * 128:(hh + 1) * 128], start=True, stop=True)))(h, acol),
                waits=[("dv", c_wt)])
            if h % 2 == 0:
                c_ae = op("vector", (lambda hh, cl: (lambda eng: eng.tensor_copy(
                    attn_sb[0:64, (hh // 2) * 128:(hh // 2 + 1) * 128],
                    ps_big[0:64, cl:cl + 128])))(h, acol), waits=[("mm", mm_a)])
            else:
                c_ae = op("vector", (lambda cl: (lambda eng: eng.tensor_copy(
                    atmp[:, :], ps_big[0:64, cl:cl + 128])))(acol),
                    waits=[("mm", mm_a)] + ([("gq", shift_dmas[-1])] if shift_dmas else []))
                shift_dmas.append(
                    dma("gpsimd", attn_sb[64:128, (h // 2) * 128:(h // 2 + 1) * 128],
                        atmp[:, :], waits=[("dv", c_ae)]))
            attn_tail.append(c_ae)
        c_attn = p.at("dv")
        g_attn = p.at("gq")

        for H in range(2):
            wg_cast = []
            for kc in range(KC):
                if kc < 2:
                    w = [("dv", p.at("dv"))] if H == 0 else [("mm", p.at("mm"))]
                else:
                    w = [("dv", wg_cast[kc - 2])]
                d = dma("sync", (wfa, wfb)[kc % 2][:, 0:512],
                        wg[kc * 128:(kc + 1) * 128, H * 512:(H + 1) * 512], waits=w)
                cv = op("vector", (lambda k2: (lambda eng: eng.tensor_copy(
                    big[:, k2 * 512:(k2 + 1) * 512],
                    (wfa, wfb)[k2 % 2][:, 0:512])))(kc), waits=[("sq", d)])
                wg_cast.append(cv)
            c_wgh = p.at("dv")

            gt_done = []
            for t in range(8):
                gps = (ps_b2, ps_b3)[t % 2]

                def gmm(tt, gp):
                    def f(eng):
                        r = None
                        for kc in range(KC):
                            r = eng.matmul(gp[:, 0:512],
                                           xs_r[:, kc * NT + tt * 128:kc * NT + (tt + 1) * 128],
                                           big[:, kc * 512:(kc + 1) * 512],
                                           start=(kc == 0), stop=(kc == KC - 1))
                        return r
                    return f
                w = [("dv", c_wgh)]
                if t >= 2:
                    w.append(("dv", gt_done[t - 2]))
                elif H == 0:
                    w.append(("dv", c_attn))
                mm_g = op("tensor", gmm(t, gps), waits=w)

                def gsm1(gp):
                    def f(eng):
                        eng.tensor_reduce(mx8[:, 0:8].rearrange("p (s o) -> p s o", o=1),
                                          gp[:, 0:512].rearrange("p (s t) -> p s t", s=8),
                                          Ax.X, Alu.max)
                        eng.drain()
                        return eng.tensor_scalar(nm8[:, 0:8], mx8[:, 0:8], -SCALE, None,
                                                 Alu.mult)
                    return f
                c_g1 = op("vector", gsm1(gps), waits=[("mm", mm_g)])

                def gexp(gp):
                    def f(eng):
                        r = None
                        for s in range(8):
                            r = eng.activation(scr1[:, s * 64:(s + 1) * 64],
                                               gp[:, s * 64:(s + 1) * 64], Act.Exp,
                                               bias=nm8[:, s:s + 1], scale=SCALE)
                        return r
                    return f
                a_ge = op("scalar", gexp(gps), waits=[("dv", c_g1)])

                def gsum(eng):
                    eng.tensor_reduce(sm8[:, 0:8].rearrange("p (s o) -> p s o", o=1),
                                      scr1[:, 0:512].rearrange("p (s t) -> p s t", s=8),
                                      Ax.X, Alu.add)
                    eng.drain()
                    return eng.reciprocal(rs8[:, 0:8], sm8[:, 0:8])
                c_gs = op("vector", gsum, waits=[("ac", a_ge)])

                def gnorm(eng):
                    r = None
                    for s in range(8):
                        r = eng.activation(scr1[:, 512 + s * 64:512 + (s + 1) * 64],
                                           scr1[:, s * 64:(s + 1) * 64], Act.Copy,
                                           scale=rs8[:, s:s + 1])
                    return r
                a_gn = op("scalar", gnorm, waits=[("dv", c_gs)])

                last = None
                for p2 in range(4):
                    pair = H * 4 + p2
                    tcol = (p2 % 2) * 512
                    mm_tr = op("tensor", (lambda p3, tc: (lambda eng: eng.matmul(
                        ps_b1[0:128, tc:tc + 128], scr1[:, 512 + p3 * 128:512 + (p3 + 1) * 128],
                        idft[:, :], start=True, stop=True, is_transpose=True)))(p2, tcol),
                        waits=[("ac", a_gn)] if p2 == 0 else [("dv", p.at("dv"))])
                    last = op("vector", (lambda pr, tc, tt: (lambda eng: eng.tensor_copy(
                        gFM[:, pr * NT + tt * 128:pr * NT + (tt + 1) * 128],
                        ps_b1[0:128, tc:tc + 128])))(pair, tcol, t),
                        waits=[("mm", mm_tr)])
                gt_done.append(last)
        c_gates = p.at("dv")

        mix_evac = []
        for h in range(HC):
            pair, row = h // 2, (h % 2) * 64
            for half in range(2):
                mps = (ps_b2, ps_b3)[half]
                w = []
                if h == 0 and half == 0:
                    w = [("dv", c_gates), ("gq", g_attn)]
                if len(mix_evac) >= 2:
                    w.append(("ac", mix_evac[-2]))

                def mmix(hh, pr, rw, hf, mp):
                    def f(eng):
                        return eng.matmul(
                            mp[:, 0:512], attn_sb[rw:rw + 64, pr * 128:(pr + 1) * 128],
                            gFM[rw:rw + 64, pr * NT + hf * NH:pr * NT + (hf + 1) * NH],
                            start=True, stop=True)
                    return f
                mm_m = op("tensor", mmix(h, pair, row, half, mps), waits=w)
                a_me = op("scalar", (lambda hh, hf, mp: (lambda eng: eng.activation(
                    big[:, hh * NT + hf * NH:hh * NT + (hf + 1) * NH], mp[:, 0:512],
                    Act.Copy)))(h, half, mps), waits=[("mm", mm_m)])
                mix_evac.append(a_me)
        a_mix = p.at("ac")

        mm_p = 0
        for oc in range(KC):
            wqv, cv = stream_slab(wp[oc * 128:(oc + 1) * 128, :], 2048, oc % 2, False,
                                  extra_waits=[("ac", a_mix)] if oc < 2 else ())
            for half in range(2):
                pps = (ps_b2, ps_b3)[half]

                def pmm(wt, hf, pp):
                    def f(eng):
                        r = None
                        for kc in range(KC):
                            r = eng.matmul(pp[:, 0:512], wt[:, kc * 128:(kc + 1) * 128],
                                           big[:, kc * NT + hf * NH:kc * NT + (hf + 1) * NH],
                                           start=(kc == 0), stop=(kc == KC - 1))
                        return r
                    return f
                w = [("dv", cv)]
                if oc > 0 or half > 0:
                    w.append(("dv", p.at("dv")))
                mm_p = op("tensor", pmm(wqv, half, pps), waits=w)
                op("vector", (lambda o2, hf, pp: (lambda eng: eng.tensor_tensor(
                    x_r[:, o2 * NT + hf * NH:o2 * NT + (hf + 1) * NH],
                    x_r[:, o2 * NT + hf * NH:o2 * NT + (hf + 1) * NH],
                    pp[:, 0:512], Alu.add)))(oc, half, pps), waits=[("mm", mm_p)])
            slab_used(oc % 2, mm_p)
        c_wp = p.at("dv")

        mm_d = 0
        for g in range(NG):
            for j in range(GSZ):
                ic = g * GSZ + j
                wqg, cvg = stream_slab(wgate[ic * 128:(ic + 1) * 128, :], 2048, 0, True,
                                       extra_waits=[("dv", c_wp)] if (g == 0 and j == 0) else ())

                def gmm2(wt):
                    def f(eng):
                        r = None
                        for half in range(2):
                            for kc in range(KC):
                                r = eng.matmul(ps_big[:, half * NH:(half + 1) * NH],
                                               wt[:, kc * 128:(kc + 1) * 128],
                                               xs_r[:, kc * NT + half * NH:kc * NT + (half + 1) * NH],
                                               start=(kc == 0), stop=(kc == KC - 1))
                        return r
                    return f
                w = [("dv", cvg)]
                if not (g == 0 and j == 0):
                    w.append(("ac", p.at("ac")))
                mm_gg = op("tensor", gmm2(wqg), waits=w)
                slab_used(0, mm_gg)

                wqu, cvu = stream_slab(wup[ic * 128:(ic + 1) * 128, :], 2048, 1, True)

                def umm(wt):
                    def f(eng):
                        r = None
                        for half in range(2):
                            for kc in range(KC):
                                r = eng.matmul(ps_big[:, 1024 + half * NH:1024 + (half + 1) * NH],
                                               wt[:, kc * 128:(kc + 1) * 128],
                                               xs_r[:, kc * NT + half * NH:kc * NT + (half + 1) * NH],
                                               start=(kc == 0), stop=(kc == KC - 1))
                        return r
                    return f
                mm_uu = op("tensor", umm(wqu), waits=[("dv", cvu)])
                slab_used(1, mm_uu)

                for half in range(2):
                    def epi(icc, hf):
                        def f(eng):
                            eng.activation(scr1[:, 0:NH], ps_big[:, hf * NH:(hf + 1) * NH],
                                           Act.Silu, scale=gs_t[:, icc:icc + 1])
                            return eng.activation(scr1[:, NH:NT],
                                                  ps_big[:, 1024 + hf * NH:1024 + (hf + 1) * NH],
                                                  Act.Copy, scale=us_t[:, icc:icc + 1])
                        return f
                    w_ep = [("mm", mm_uu)]
                    if half == 1:
                        w_ep.append(("dv", p.at("dv")))
                    a_ep = op("scalar", epi(ic, half), waits=w_ep)
                    op("vector", (lambda jj, hf: (lambda eng: eng.tensor_tensor(
                        big[:, jj * NT + hf * NH:jj * NT + (hf + 1) * NH],
                        scr1[:, 0:NH], scr1[:, NH:NT], Alu.mult)))(j, half),
                        waits=[("ac", a_ep)])
            c_fuse = p.at("dv")

            for oc in range(KC):
                wqd, cvd = stream_slab(
                    wdown[(g * 16 + oc) * 128:(g * 16 + oc + 1) * 128, :],
                    GSZ * 128, oc % 2, True,
                    extra_waits=[("dv", c_fuse)] if oc < 2 else ())
                for half in range(2):
                    def dmmf(wt, hf):
                        def f(eng):
                            r = None
                            for j in range(GSZ):
                                r = eng.matmul(ps_b1[:, hf * NH:(hf + 1) * NH],
                                               wt[:, j * 128:(j + 1) * 128],
                                               big[:, j * NT + hf * NH:j * NT + (hf + 1) * NH],
                                               start=(j == 0), stop=(j == GSZ - 1))
                            return r
                        return f
                    w = [("dv", cvd)]
                    if oc > 0 or half > 0 or g > 0:
                        w.append(("dv", p.at("dv")))
                    mm_d = op("tensor", dmmf(wqd, half), waits=w)
                    a_de = op("scalar", (lambda o2, hf: (lambda eng: eng.activation(
                        scr1[:, hf * NH:(hf + 1) * NH], ps_b1[:, hf * NH:(hf + 1) * NH],
                        Act.Copy, scale=ds_t[:, o2:o2 + 1])))(oc, half),
                        waits=[("mm", mm_d)])
                    op("vector", (lambda o2, hf: (lambda eng: eng.tensor_tensor(
                        x_r[:, o2 * NT + hf * NH:o2 * NT + (hf + 1) * NH],
                        x_r[:, o2 * NT + hf * NH:o2 * NT + (hf + 1) * NH],
                        scr1[:, hf * NH:(hf + 1) * NH], Alu.add)))(oc, half),
                        waits=[("ac", a_de)])
                slab_used(oc % 2, mm_d)
        c_ffn = p.at("dv")

        for c in range(KC):
            dma("gpsimd", out[c * 128:(c + 1) * 128, :], x_r[:, c * NT:(c + 1) * NT],
                waits=[("dv", c_ffn)] if c == 0 else ())
        p.add("gpsimd", lambda eng: None, waits=[("gq", p.at("gq"))])

        sems = {}
        with contextlib.ExitStack() as sst:
            for s in ("sq", "gq", "dv", "ac", "mm"):
                sems[s] = sst.enter_context(nc.semaphore(s))
            with nc.Block() as block:
                def make(eng_name):
                    def stream(eng):
                        lastw = {}
                        for (e, waits, fn, incs) in p.ops:
                            if e != eng_name:
                                continue
                            for (s, v) in waits:
                                if v > 0 and lastw.get(s, -1) < v:
                                    eng.wait_ge(sems[s], v)
                                    lastw[s] = v
                            instr = fn(eng)
                            for (s, amt) in incs:
                                instr.then_inc(sems[s], amt)
                    return stream
                block.sync(make("sync"))
                block.gpsimd(make("gpsimd"))
                block.scalar(make("scalar"))
                block.vector(make("vector"))
                block.tensor(make("tensor"))
    return nc


def _host_layouts(inputs):
    x = np.asarray(inputs["L0_attn_x"], dtype=np.float32)
    Wk = np.asarray(inputs["L0_attn_Wk"], dtype=np.float32)
    Wv = np.asarray(inputs["L0_attn_Wv"], dtype=np.float32)
    Wg = np.asarray(inputs["L0_attn_Wg"], dtype=np.float32)
    Wp = np.asarray(inputs["L0_attn_Wp"], dtype=np.float32)
    gw = np.asarray(inputs["L0_ffn_gate_w"], dtype=np.float32)
    uw = np.asarray(inputs["L0_ffn_up_w"], dtype=np.float32)
    dw = np.asarray(inputs["L0_ffn_down_w"], dtype=np.float32)
    gs = np.asarray(inputs["L0_ffn_gate_scale"], dtype=np.float32)
    us = np.asarray(inputs["L0_ffn_up_scale"], dtype=np.float32)
    ds = np.asarray(inputs["L0_ffn_down_scale"], dtype=np.float32)
    lq = np.asarray(inputs["L0_attn_latent_queries"], dtype=np.float32)

    perm_d = np.concatenate([np.arange(0, 128, 2), np.arange(1, 128, 2)])

    Wk_p = Wk.reshape(C, HC, D)[:, :, perm_d].reshape(C, C)
    wk_r = np.ascontiguousarray(
        Wk_p.reshape(KC, 128, HC, D).transpose(2, 1, 0, 3).reshape(C, C))
    wv_r = np.ascontiguousarray(
        Wv.reshape(KC, 128, HC, D).transpose(2, 1, 0, 3).reshape(C, C))
    wp_r = np.ascontiguousarray(
        Wp.reshape(KC, 128, KC, 128).transpose(2, 1, 0, 3).reshape(C, C))
    wgate_r = np.ascontiguousarray(
        gw.reshape(KC, 128, IC, 128).transpose(2, 1, 0, 3).reshape(IC * 128, C))
    wup_r = np.ascontiguousarray(
        uw.reshape(KC, 128, IC, 128).transpose(2, 1, 0, 3).reshape(IC * 128, C))
    dwr = dw.reshape(NG, GSZ, 128, KC, 128).transpose(0, 3, 2, 1, 4)
    wdown_r = np.ascontiguousarray(dwr.reshape(NG * KC * 128, GSZ * 128))
    lq_r = np.ascontiguousarray(
        lq[0].transpose(1, 2, 0)[:, perm_d, :].transpose(1, 0, 2).reshape(128, HC * L))

    import jax
    import jax.numpy as jnp
    cpu = jax.devices("cpu")[0]
    with jax.default_device(cpu):
        invf = 1.0 / 10000.0 ** (jnp.arange(0, 64, dtype=jnp.float32) * 2.0 / 128.0)
        ang = jnp.arange(128, dtype=jnp.float32)[:, None] * invf[None, :]
        cosv = np.asarray(jnp.cos(ang), dtype=np.float32)
        sinv = np.asarray(jnp.sin(ang), dtype=np.float32)
    rCh = np.empty((128, 128), np.float32)
    rSh = np.empty((128, 128), np.float32)
    rCh[0:64, :] = cosv.T
    rCh[64:128, :] = cosv.T
    rSh[0:64, :] = -sinv.T
    rSh[64:128, :] = sinv.T

    maska = np.where(np.arange(128)[None, :] <= np.arange(64)[:, None],
                     np.float32(0.0), np.float32(NEG)).astype(np.float32)
    idf = np.eye(128, dtype=np.float32)
    gs_r = np.ascontiguousarray(gs.reshape(IC, 128).T)
    us_r = np.ascontiguousarray(us.reshape(IC, 128).T)
    ds_r = np.ascontiguousarray(ds.reshape(KC, 128).T)

    shared = dict(wk=wk_r, wv=wv_r, wg=np.ascontiguousarray(Wg), wp=wp_r,
                  wgate=wgate_r, wup=wup_r, wdown=wdown_r, lq=lq_r,
                  ropec=rCh, ropes=rSh, maska=maska, idf=idf,
                  gs=gs_r, us=us_r, ds=ds_r)
    in_maps = []
    for c in range(8):
        b, q = c // 4, c % 4
        m = dict(shared)
        m["xT"] = np.ascontiguousarray(x[b, q * NT:(q + 1) * NT, :].T)
        m["xhT"] = np.ascontiguousarray(x[b, 0:128, :].T)
        in_maps.append(m)
    return in_maps


def kernel(**inputs):
    in_maps = _host_layouts(inputs)
    nc = build()
    res = run_bass_kernel_spmd(nc, in_maps, core_ids=list(range(8)))
    x = np.asarray(inputs["L0_attn_x"])
    outp = np.empty((2, 4096, C), dtype=np.float32)
    for c in range(8):
        b, q = c // 4, c % 4
        outp[b, q * NT:(q + 1) * NT, :] = np.asarray(res.results[c]["out"]).T
    return outp
